# version 15
# baseline (speedup 1.0000x reference)
import contextlib
import math
import numpy as np
import concourse.bass as bass
import concourse.mybir as mybir
from concourse.bass_utils import run_bass_kernel_spmd
from concourse.aot_env import allow_ici_all_to_all
import ml_dtypes

F32 = mybir.dt.float32
BF16 = mybir.dt.bfloat16
I8 = mybir.dt.int8
AF = mybir.ActivationFunctionType
ALU = mybir.AluOpType
AXX = mybir.AxisListType.X

NCORES = 8


class Cfg:
    def __init__(self, B=4, S=2048, D=4096, HEADS=16, HD=256, ROT=64):
        self.B, self.S, self.D, self.HEADS, self.HD, self.ROT = B, S, D, HEADS, HD, ROT
        self.T = B * S
        self.TO = self.T // NCORES
        self.HPC = HEADS // NCORES
        self.DH = self.HPC * HD
        self.EC = D // 128
        self.TBN = self.T // 128
        self.UB = S // 128
        self.KC = min(512, S)
        self.QC = min(512, S)
        self.NKC = S // self.KC
        self.NQC = S // self.QC
        self.DC = HD // 128
        self.OCN = D // 512
        self.OTB = self.TO // 128
        self.NF = ROT // 2
        self.TC = min(512, self.TO)
        self.NCH = self.TO // self.TC
        self.NTC = self.T // self.TC
        self.NUN = self.B * self.HPC


FULL = Cfg()
KNOB = {'rope': True, 'norms': True, 'spill': True, 'kmax': True, 'qkv': True, 'maxtc': 10**9, 'ropebanks': 2, 'nop0': False, 'nosq': False, 'nokb': False, 'p3max': 4, 'xsync': False, 'noat': False, 'sev': 0}


class EW:

    def __init__(self, eng, semh, ct, key):
        self.e, self.s, self.ct, self.key = eng, semh, ct, key

    def n(self):
        return self.ct.get(self.key, 0)

    def wait(self, semh, val):
        if val > 0:
            self.e.wait_ge(semh, val)

    def __getattr__(self, name):
        f = getattr(self.e, name)
        ew = self

        def call(*a, **k):
            if ew.n() > 0:
                ew.e.wait_ge(ew.s, ew.n())
            instr = f(*a, **k)
            instr.then_inc(ew.s, 1)
            ew.ct[ew.key] = ew.n() + 1
            return instr
        return call


def build_program(c, debug=False, phases=3):
    nc = bass.Bass("TRN2", target_bir_lowering=False)
    ct = {}

    def inc(key, n=1):
        ct[key] = ct.get(key, 0) + n
        return ct[key]

    def get(key):
        return ct.get(key, 0)

    okind = "ExternalOutput" if debug else "Internal"

    hs = nc.dram_tensor("hs", [c.TO, c.D], F32, kind="ExternalInput").ap()
    hsf = nc.dram_tensor("hsf", [c.T, c.D], F32, kind="ExternalInput").ap()
    wqT = nc.dram_tensor("wqT", [c.D, c.DH], F32, kind="ExternalInput").ap()
    wkT = nc.dram_tensor("wkT", [c.D, c.DH], F32, kind="ExternalInput").ap()
    wvT = nc.dram_tensor("wvT", [c.D, c.DH], F32, kind="ExternalInput").ap()
    woT = nc.dram_tensor("woT", [c.D, c.D], F32, kind="ExternalInput").ap()
    wfT = nc.dram_tensor("wfT", [c.D, c.D], I8, kind="ExternalInput").ap()
    scl = nc.dram_tensor("scl", [1, c.D], F32, kind="ExternalInput").ap()
    cosd = nc.dram_tensor("cosd", [c.S, c.HPC * c.NF], F32, kind="ExternalInput").ap()
    sind = nc.dram_tensor("sind", [c.S, c.HPC * c.NF], F32, kind="ExternalInput").ap()
    m4d = nc.dram_tensor("m4d", [c.KC // 128, 128, c.KC], BF16, kind="ExternalInput").ap()
    out = nc.dram_tensor("out", [c.TO, c.D], F32, kind="ExternalOutput").ap()

    Qn = nc.dram_tensor("Qn", [c.T, c.DH], BF16, kind=okind)
    Kn = nc.dram_tensor("Kn", [c.T, c.DH], BF16, kind=okind)
    Vn = nc.dram_tensor("Vn", [c.T, c.DH], BF16, kind=okind)
    a2a_in = nc.dram_tensor("a2a_in", [NCORES, c.DH, c.TO], BF16)
    a2a_out = nc.dram_tensor("a2a_out", [NCORES, c.DH, c.TO], BF16)
    z1d = nc.dram_tensor("z1d", [c.TO, c.D], F32, kind=okind)
    x1d = nc.dram_tensor("x1d", [c.TO, c.D], F32, kind=okind)
    z2d = nc.dram_tensor("z2d", [c.TO, c.D], F32)
    if debug:
        a2ad = nc.dram_tensor("a2ad", [NCORES, c.DH, c.TO], BF16, kind="ExternalOutput").ap()
        a2apre = nc.dram_tensor("a2apre", [NCORES, c.DH, c.TO], BF16, kind="ExternalOutput").ap()
        dledd = nc.dram_tensor("dledd", [128, 3 * c.NUN * c.UB], F32, kind="ExternalOutput").ap()

    with contextlib.ExitStack() as st:
        def sb(stk, name, shape, dt):
            return stk.enter_context(nc.sbuf_tensor(name, shape, dt))

        def sem(name):
            return st.enter_context(nc.semaphore(name))

        blk = st.enter_context(nc.Block())

        junk = sb(st, "junk", [128, 264], F32)
        zsum = sb(st, "zsum", [128, c.OTB], F32)
        zsq = sb(st, "zsq", [128, c.OTB], F32)
        meanc = sb(st, "meanc", [128, c.OTB], F32)
        rsig = sb(st, "rsig", [128, c.OTB], F32)
        tmpc = sb(st, "tmpc", [128, c.OTB], F32)
        sqcol = sb(st, "sqcol", [128, 2], F32)
        dled = sb(st, "dled", [128, 3 * c.NUN * c.UB], F32) if debug else None

        psb = [st.enter_context(nc.psum_tensor(f"pb{i}", [128, 512], F32)) for i in range(8)]

        t_all = sem("t_all")
        v_all = sem("v_all")
        s_all = sem("s_all")
        g_pre = sem("g_pre")
        g_cc = sem("g_cc")
        g_w = [sem(f"g_w{i}") for i in range(3)]
        sy_p0h = [sem("sy_p0a"), sem("sy_p0b")]
        sy_h = [sem(f"sy_h{i}") for i in range(2)]
        sy_sp = sem("sy_sp")
        sy_km = sem("sy_km")
        g_hs = [sem("g_hs0"), sem("g_hs1")]
        sy_u = [sem(f"sy_u{i}") for i in range(2)]
        sy_tr = [sem(f"sy_tr{i}") for i in range(2)]
        sy_pt = [sem(f"sy_pt{i}") for i in range(2)]
        sy_a2 = sem("sy_a2")
        sy_at = [sem(f"sy_at{i}") for i in range(2)]
        NSL = 4
        scx = [sem(f"scx{i}") for i in range(NSL)]
        szs = [sem(f"szs{i}") for i in range(NSL)]
        sy_xb = sem("sy_xb")

        def TW(t):
            return EW(t, t_all, ct, "t_all")

        def VW(v):
            return EW(v, v_all, ct, "v_all")

        def SW(s):
            return EW(s, s_all, ct, "s_all")

        xt_guard = {}
        zt_guard = {}

        p1 = contextlib.ExitStack()
        cosT = sb(p1, "cosT", [128, c.UB * c.HPC * c.NF], F32)
        sinT = sb(p1, "sinT", [128, c.UB * c.HPC * c.NF], F32)
        wq_sb = sb(p1, "wq_sb", [128, c.EC * c.DH], BF16)
        wk_sb = sb(p1, "wk_sb", [128, c.EC * c.DH], BF16)
        wv_sb = sb(p1, "wv_sb", [128, c.EC * c.DH], BF16)
        hst = [sb(p1, f"hst{i}", [128, c.EC * c.TC], BF16) for i in range(2)]
        natb = sb(p1, "natb", [128, 2 * c.D], BF16)
        qstg = sb(p1, "qstg", [128, 4 * c.DH], BF16)
        kstg = sb(p1, "kstg", [128, 4 * c.DH], BF16)
        vstg = sb(p1, "vstg", [128, 4 * c.DH], BF16)
        rt = sb(p1, "rt", [128, 4 * c.HPC * c.NF], F32)

        if debug:
            @blk.vector
            def _(v):
                vw = VW(v)
                vw.memset(dled[:], 0.0)

        @blk.gpsimd
        def _(g):
            for wsrc, wdst in ((wqT, wq_sb), (wkT, wk_sb), (wvT, wv_sb)):
                g.dma_start(out=wdst[:].rearrange("p (e d) -> p e d", e=c.EC),
                            in_=wsrc.rearrange("(e p) d -> p e d", p=128)).then_inc(g_pre, 16)
                inc("g_pre", 16)
            g.dma_start(out=cosT[:].rearrange("p (a f) -> p a f", a=c.UB),
                        in_=cosd.rearrange("(a p) f -> p a f", p=128)).then_inc(g_pre, 16)
            inc("g_pre", 16)
            g.dma_start(out=sinT[:].rearrange("p (a f) -> p a f", a=c.UB),
                        in_=sind.rearrange("(a p) f -> p a f", p=128)).then_inc(g_pre, 16)
            inc("g_pre", 16)
        prep_done = get("g_pre")

        if phases < 1:
            @blk.sync
            def _(sy):
                sy.wait_ge(g_cc, get("g_cc"))
            p1.close()
            return nc, ct

        tbpc = c.TC // 128

        def emit_hst_load(tc):
            slot = tc % 2
            nblk = c.TC // 128
            for s in range(nblk):
                b = s % 2

                @blk.gpsimd
                def _(g, tc=tc, s=s, b=b):
                    guard = ct.get(f"natb_tr{b}")
                    if guard is not None:
                        g.wait_ge(sy_p0h[b], guard * 16)
                    g.dma_start(
                        out=natb[:].rearrange("p (b2 e) -> p b2 e", b2=2)[:, b],
                        in_=hsf[tc * c.TC + s * 128:tc * c.TC + (s + 1) * 128, :],
                    ).then_inc(g_hs[b], 16)
                    inc(f"g_hs{b}", 16)
                cast_done = get(f"g_hs{b}")

                @blk.sync
                def _(sy, tc=tc, slot=slot, s=s, b=b, cast_done=cast_done):
                    sy.wait_ge(g_hs[b], cast_done)
                    if s == 0 and tc >= 2:
                        sy.wait_ge(t_all, ct[f"qkv_tc{tc - 2}"])
                    sy.dma_start_transpose(
                        out=hst[slot][:].rearrange("p (e a f) -> p e a f", e=c.EC,
                                                   a=nblk)[:, :, s, :],
                        in_=natb[:].rearrange("p (b2 e f) -> p b2 e f", b2=2,
                                              e=c.EC)[:, b],
                    ).then_inc(sy_p0h[b], 16)
                    inc(f"sy_p0{b}")
                ct[f"natb_tr{b}"] = get(f"sy_p0{b}")
            ct[f"hst_tc{tc}"] = [(sy_p0h[hh], ct.get(f"natb_tr{hh}", 0) * 16)
                                 for hh in range(min(2, nblk))]

        emit_hst_load(0)
        if c.NTC > 1:
            emit_hst_load(1)

        wview = [wq_sb[:].rearrange("p (e d) -> p e d", e=c.EC),
                 wk_sb[:].rearrange("p (e d) -> p e d", e=c.EC),
                 wv_sb[:].rearrange("p (e d) -> p e d", e=c.EC)]

        for tc in range(min(c.NTC, KNOB['maxtc'])):
            slot = tc % 2
            for j in range(tbpc):
                gtb = tc * tbpc + j
                bq, bk, bv = (0, 1, 2) if gtb % 2 == 0 else (3, 4, 5)
                stg = gtb % 4

                @blk.tensor
                def _(t, tc=tc, slot=slot, j=j, gtb=gtb, bq=bq, bk=bk, bv=bv):
                    tw = TW(t)
                    if gtb == 0:
                        tw.wait(g_pre, prep_done)
                    if j == 0:
                        for sm, val in ct[f"hst_tc{tc}"]:
                            tw.wait(sm, val)
                    if gtb >= 2:
                        tw.wait(v_all, ct[f"ev_v_tb{gtb - 2}"])
                    hv = hst[slot][:].rearrange("p (e t) -> p e t", e=c.EC)
                    for e in range(c.EC):
                        lhs = hv[:, e, j * 128:(j + 1) * 128]
                        st_, sp_ = (e == 0), (e == c.EC - 1)
                        t.matmul(psb[bq][:, 0:c.DH], lhs, wview[0][:, e, :], start=st_, stop=sp_)
                        t.matmul(psb[bk][:, 0:c.DH], lhs, wview[1][:, e, :], start=st_, stop=sp_)
                        mm = t.matmul(psb[bv][:, 0:c.DH], lhs, wview[2][:, e, :], start=st_, stop=sp_)
                    mm.then_inc(t_all, 1)
                    inc("t_all", 1)
                ct[f"qkv_tb{gtb}"] = get("t_all")
                if j == tbpc - 1:
                    ct[f"qkv_tc{tc}"] = get("t_all")

                @blk.vector
                def _(v, gtb=gtb, bq=bq, bk=bk, bv=bv, stg=stg):
                    vw = VW(v)
                    vw.wait(t_all, ct[f"qkv_tb{gtb}"])
                    grp = gtb // 4
                    if grp >= 1 and stg == 0 and KNOB['spill']:
                        vw.wait(sy_sp, 48 * grp)
                    ub = gtb % c.UB
                    ctab = cosT[:].rearrange("p (a h f) -> p a h f", a=c.UB, h=c.HPC)[:, ub]
                    stab = sinT[:].rearrange("p (a h f) -> p a h f", a=c.UB, h=c.HPC)[:, ub]
                    rv = rt[:].rearrange("p (i h f) -> p i h f", i=4, h=c.HPC)
                    for bi, (bank, outstg) in enumerate(((bq, qstg), (bk, kstg))):
                        pv = psb[bank][:, 0:c.DH].rearrange("p (h d) -> p h d", h=c.HPC)
                        ov = outstg[:].rearrange("p (s h d) -> p s h d", s=4, h=c.HPC)[:, stg]
                        if KNOB['rope'] and bi < KNOB['ropebanks']:
                            lvl = 3 if KNOB['rope'] is True else KNOB['rope']
                            ev, od = pv[:, :, 0:c.ROT:2], pv[:, :, 1:c.ROT:2]
                            oev, ood = ov[:, :, 0:c.ROT:2], ov[:, :, 1:c.ROT:2]
                            sel = KNOB.get('ropesel', [0, 1, 2, 3])
                            ops = [(0, ev, ctab), (1, od, stab), (2, od, ctab), (3, ev, stab)]
                            for ii, aa, bb in ops:
                                if ii in sel and (ii == 0 or lvl >= 2):
                                    vw.tensor_tensor(rv[:, ii], aa, bb, op=ALU.mult)
                            if lvl >= 3:
                                vw.tensor_tensor(oev, rv[:, 0], rv[:, 1], op=ALU.subtract)
                                vw.tensor_tensor(ood, rv[:, 2], rv[:, 3], op=ALU.add)
                            vw.tensor_copy(ov[:, :, 0:c.ROT] if lvl < 3 else ov[:, :, c.ROT:],
                                           pv[:, :, 0:c.ROT] if lvl < 3 else pv[:, :, c.ROT:])
                            if lvl < 3 and c.HD > c.ROT:
                                vw.tensor_copy(ov[:, :, c.ROT:], pv[:, :, c.ROT:])
                        else:
                            vw.tensor_copy(outstg[:, stg * c.DH:(stg + 1) * c.DH],
                                           psb[bank][:, 0:c.DH])
                    vw.tensor_copy(vstg[:, stg * c.DH:(stg + 1) * c.DH], psb[bv][:, 0:c.DH])
                ct[f"ev_v_tb{gtb}"] = get("v_all")

                if stg == 3 and KNOB['spill']:
                    grp = gtb // 4

                    @blk.sync
                    def _(sy, grp=grp, gtb=gtb):
                        sy.wait_ge(v_all, ct[f"ev_v_tb{gtb}"])
                        for stg_t, hbm_t in ((qstg, Qn), (kstg, Kn), (vstg, Vn)):
                            sy.dma_start(
                                out=hbm_t[grp * 512:(grp + 1) * 512, :]
                                .rearrange("(a p) d -> p a d", p=128),
                                in_=stg_t[:].rearrange("p (a d) -> p a d", a=4)
                            ).then_inc(sy_sp, 16)
                            inc("sy_sp", 16)
                if tc + 2 < c.NTC and j == tbpc - 1:
                    emit_hst_load(tc + 2)
        p1_spills = get("sy_sp")

        p1.close()
        p2 = contextlib.ExitStack()
        m4 = sb(p2, "m4", [128, (c.KC // 128) * c.KC], BF16)
        qT = [sb(p2, f"qT{i}", [128, c.DC * c.S], BF16) for i in range(2)]
        kT = [sb(p2, f"kT{i}", [128, c.DC * c.S], BF16) for i in range(2)]
        vU = [sb(p2, f"vU{i}", [128, c.UB * c.HD], BF16) for i in range(2)]
        nstg = [sb(p2, f"nstg{i}", [128, c.UB * c.HD], BF16) for i in range(2)]
        pnat = [sb(p2, f"pnat{i}", [128, c.S], BF16) for i in range(2)]
        sst = [sb(p2, f"sst{i}", [128, c.S], F32) for i in range(2)]
        pT = [sb(p2, f"pT{i}", [128, (c.S // 128) * c.QC], BF16) for i in range(2)]
        avst = sb(p2, "avst", [128, c.DC * c.QC], BF16)
        lsum = [sb(p2, f"lsum{i}", [128, 4 + c.NKC], F32) for i in range(2)]

        @blk.gpsimd
        def _(g):
            g.wait_ge(sy_sp, p1_spills)
            g.dma_start(out=m4[:].rearrange("p (a k) -> p a k", a=c.KC // 128),
                        in_=m4d.rearrange("a p k -> p a k")).then_inc(g_pre, 16)
            inc("g_pre", 16)
        m4_ready = get("g_pre")

        for u in range(c.NUN):
            b, h = u // c.HPC, u % c.HPC
            par = u % 2
            su, str_, sun, strn = sy_u[par], sy_tr[par], f"sy_u{par}", f"sy_tr{par}"

            @blk.sync
            def _(sy, b=b, h=h, par=par, su=su, str_=str_, sun=sun, strn=strn, u=u):
                sy.wait_ge(sy_sp, p1_spills)
                if u >= 2:
                    sy.wait_ge(t_all, ct[f"unit_t{u - 2}"])
                for src_t, dst in ((Kn, kT), (Qn, qT)):
                    for dc_ in range(c.DC):
                        sy.dma_start(
                            out=nstg[par][:].rearrange("p (dc a f) -> p dc a f",
                                                       dc=c.DC, a=c.UB)[:, dc_],
                            in_=src_t[b * c.S:(b + 1) * c.S,
                                      h * c.HD + dc_ * 128:h * c.HD + (dc_ + 1) * 128]
                            .rearrange("(a p) f -> p a f", p=128)).then_inc(su, 16)
                        inc(sun, 16)
                    sy.wait_ge(su, get(sun))
                    for dc_ in range(c.DC):
                        sy.dma_start_transpose(
                            out=dst[par][:].rearrange("p (dc a f) -> p dc a f", dc=c.DC,
                                                      a=c.UB)[:, dc_],
                            in_=nstg[par][:].rearrange("p (dc a f) -> p dc a f", dc=c.DC,
                                                       a=c.UB)[:, dc_],
                        ).then_inc(str_, 16)
                        inc(strn, 16)
                    sy.wait_ge(str_, get(strn))
                sy.dma_start(out=vU[par][:],
                             in_=Vn[b * c.S:(b + 1) * c.S, h * c.HD:(h + 1) * c.HD]
                             .rearrange("(a p) d -> p a d", p=128)).then_inc(su, 16)
                inc(sun, 16)
            u_loaded = get(sun)
            tr_loaded = get(strn)

            for qc in range(c.NQC):
                gch = get("gch")
                pcp = gch % 2
                vis = qc + 1
                pend = ct.get("pend_back")
                for jj in range(c.QC // 128):
                    gqb = inc("gqb")
                    qb = qc * (c.QC // 128) + jj
                    pp = (gqb - 1) % 2

                    def emit_score_chunk(kc, pp, qb, par, bank, sst_free):
                        bank_guard = ct.get(f"sbank_free{bank}")

                        @blk.tensor
                        def _(t, qb=qb, kc=kc, bank=bank, par=par, bank_guard=bank_guard):
                            tw = TW(t)
                            if bank_guard is not None:
                                tw.wait(bank_guard[0], bank_guard[1])
                            if kc == 0:
                                tw.wait(str_, tr_loaded)
                            for dc in range(c.DC):
                                mm = t.matmul(
                                    psb[bank][:, 0:c.KC],
                                    qT[par][:, dc * c.S + qb * 128:dc * c.S + (qb + 1) * 128],
                                    kT[par][:, dc * c.S + kc * c.KC:dc * c.S + (kc + 1) * c.KC],
                                    start=(dc == 0), stop=(dc == c.DC - 1))
                            mm.then_inc(t_all, 1)
                            inc("t_all", 1)
                        t_s_here = get("t_all")

                        @blk.vector
                        def _(v, kc=kc, bank=bank, pp=pp, t_s_here=t_s_here, sst_free=sst_free):
                            vw = VW(v)
                            vw.wait(t_all, t_s_here)
                            if kc == 0 and sst_free is not None:
                                vw.wait(s_all, sst_free)
                            vw.tensor_copy(sst[pp][:, kc * c.KC:(kc + 1) * c.KC],
                                           psb[bank][:, 0:c.KC])
                        ct[f"sbank_free{bank}"] = (v_all, get("v_all"))

                    SBANKS = (6, 7, 0, 1)
                    sst_free = ct.get(f"sst_exp_pp{pp}")
                    for kc in range(vis):
                        emit_score_chunk(kc, pp, qb, par, SBANKS[(inc("sbank") - 1) % 4],
                                         sst_free if kc == 0 else None)
                    @blk.vector
                    def _(v, pp=pp, vis=vis, qb=qb):
                        vw = VW(v)
                        vw.wait(g_pre, m4_ready)
                        mi = qb % (c.KC // 128)
                        vw.tensor_tensor(sst[pp][:, (vis - 1) * c.KC:vis * c.KC],
                                         sst[pp][:, (vis - 1) * c.KC:vis * c.KC],
                                         m4[:, mi * c.KC:(mi + 1) * c.KC], op=ALU.add)
                        vw.tensor_reduce(lsum[pp][:, 0:1], sst[pp][:, 0:vis * c.KC],
                                         axis=AXX, op=ALU.max)
                        vw.tensor_scalar(sst[pp][:, 0:vis * c.KC], sst[pp][:, 0:vis * c.KC],
                                         lsum[pp][:, 0:1], None, op0=ALU.subtract)
                    bias_ok = get("v_all")

                    def make_back(gqb=gqb, pp=pp, pcp=pcp, jj=jj, vis=vis, gch=gch,
                                  bias_ok=bias_ok):
                        def back():
                            pn_guard = ct[f"pt_strip{gqb - 2}"] if gqb > 2 else None

                            @blk.scalar
                            def _(s, pp=pp, vis=vis, bias_ok=bias_ok, pn_guard=pn_guard):
                                sw = SW(s)
                                sw.wait(v_all, bias_ok)
                                if pn_guard is not None:
                                    sw.wait(sy_pt[pp], pn_guard)
                                sw.activation(pnat[pp][:, 0:vis * c.KC],
                                              sst[pp][:, 0:vis * c.KC],
                                              AF.Exp, scale=1.0 / math.sqrt(c.HD))
                            ct[f"sst_exp_pp{pp}"] = get("s_all")
                            strip_exp = get("s_all")

                            @blk.vector
                            def _(v, pp=pp, vis=vis, strip_exp=strip_exp, gqb=gqb):
                                vw = VW(v)
                                vw.wait(s_all, strip_exp)
                                vw.tensor_reduce(lsum[pp][:, 2:3], pnat[pp][:, 0:vis * c.KC],
                                                 axis=AXX, op=ALU.add)
                                vw.reciprocal(lsum[pp][:, 3:4], lsum[pp][:, 2:3])
                                if debug:
                                    vw.tensor_copy(dled[:, (gqb - 1) * 2:(gqb - 1) * 2 + 2],
                                                   lsum[pp][:, 2:4])
                                vw.tensor_scalar(pnat[pp][:, 0:vis * c.KC],
                                                 pnat[pp][:, 0:vis * c.KC],
                                                 lsum[pp][:, 3:4], None, op0=ALU.mult)
                            scale_v = get("v_all")

                            @blk.sync
                            def _(sy, pp=pp, pcp=pcp, jj=jj, vis=vis, gch=gch, scale_v=scale_v):
                                sy.wait_ge(v_all, scale_v)
                                if jj == 0 and gch >= 2:
                                    sy.wait_ge(t_all, ct[f"av_t_ch{gch - 2}"])
                                nb = vis * (c.KC // 128)
                                sy.dma_start_transpose(
                                    out=pT[pcp][:].rearrange("p (a f) -> p a f", f=c.QC)
                                    [:, 0:nb, jj * 128:(jj + 1) * 128],
                                    in_=pnat[pp][:, 0:vis * c.KC].rearrange("p (a f) -> p a f",
                                                                            a=nb),
                                ).then_inc(sy_pt[pp], 16)
                                inc(f"sy_pt{pp}", 16)
                            ct[f"pt_strip{gqb}"] = get(f"sy_pt{pp}")
                        return back

                    if pend is not None:
                        pend()
                    pend = make_back()
                if pend is not None:
                    pend()
                    pend = None
                ct["pend_back"] = None

                nkb = vis * (c.KC // 128)
                pt0_done, pt1_done = get("sy_pt0"), get("sy_pt1")

                @blk.tensor
                def _(t, gch=gch, pcp=pcp, par=par, nkb=nkb, pt0=pt0_done, pt1=pt1_done,
                      u_loaded=u_loaded, su=su):
                    tw = TW(t)
                    tw.wait(sy_pt[0], pt0)
                    tw.wait(sy_pt[1], pt1)
                    tw.wait(su, u_loaded)
                    if gch >= 2:
                        tw.wait(v_all, ct[f"av_v_ch{gch - 2}"])
                    ab = 2 if gch % 2 == 0 else 4
                    for dc in range(c.DC):
                        for kb in range(nkb):
                            mm = t.matmul(
                                psb[ab + dc][:, 0:c.QC],
                                vU[par][:, kb * c.HD + dc * 128:kb * c.HD + (dc + 1) * 128],
                                pT[pcp][:, kb * c.QC:(kb + 1) * c.QC],
                                start=(kb == 0), stop=(kb == nkb - 1))
                        mm.then_inc(t_all, 1)
                        inc("t_all", 1)
                ct[f"av_t_ch{gch}"] = get("t_all")

                @blk.vector
                def _(v, gch=gch):
                    vw = VW(v)
                    vw.wait(t_all, ct[f"av_t_ch{gch}"])
                    if gch >= 1:
                        vw.wait(sy_a2, ct[f"a2st_ch{gch - 1}"])
                    ab = 2 if gch % 2 == 0 else 4
                    for dc in range(c.DC):
                        vw.tensor_copy(avst[:, dc * c.QC:(dc + 1) * c.QC], psb[ab + dc][:, 0:c.QC])
                ct[f"av_v_ch{gch}"] = get("v_all")

                @blk.sync
                def _(sy, b=b, h=h, qc=qc, gch=gch):
                    sy.wait_ge(v_all, ct[f"av_v_ch{gch}"])
                    qg = b * c.S + qc * c.QC
                    slot, col = qg // c.TO, qg % c.TO
                    for dc in range(c.DC):
                        sy.dma_start(
                            out=a2a_in[slot, h * c.HD + dc * 128:h * c.HD + (dc + 1) * 128,
                                       col:col + c.QC],
                            in_=avst[:, dc * c.QC:(dc + 1) * c.QC]).then_inc(sy_a2, 16)
                        inc("sy_a2", 16)
                ct[f"a2st_ch{gch}"] = get("sy_a2")
                inc("gch")

            ct[f"unit_t{u}"] = get("t_all")
            ct[f"unit_s{u}"] = get("s_all")

        a2a_stores = get("sy_a2")

        if phases < 3:
            @blk.sync
            def _(sy):
                sy.wait_ge(sy_a2, a2a_stores)
            p2.close()
            return nc, ct

        if debug:
            @blk.sync
            def _(sy):
                sy.wait_ge(sy_a2, a2a_stores)
                sy.dma_start(out=a2apre, in_=a2a_in.ap()).then_inc(sy_km, 16)
                inc("sy_km", 16)

        @blk.gpsimd
        def _(g):
            g.wait_ge(sy_a2, a2a_stores)
            if debug:
                g.wait_ge(sy_km, get("sy_km"))
            with allow_ici_all_to_all():
                g.collective_compute(
                    "AllToAll", ALU.bypass, replica_groups=[list(range(NCORES))],
                    ins=[a2a_in.ap().opt()], outs=[a2a_out.ap().opt()],
                ).then_inc(g_cc, 1)
            inc("g_cc", 1)

        p2.close()
        p3 = contextlib.ExitStack()
        wring = [sb(p3, f"wring{i}", [128, c.EC * 512], BF16) for i in range(3)]
        x1bf = sb(p3, "x1bf", [128, c.D], BF16)
        sclf = sb(p3, "sclf", [128, c.D], BF16)
        xt = [sb(p3, f"xt{i}", [128, 512], F32) for i in range(NSL)]
        zt = [sb(p3, f"zt{i}", [128, 512], F32) for i in range(NSL)]
        pa = contextlib.ExitStack()
        atres = sb(pa, "atres", [128, c.EC * c.TO], BF16)
        x1T = None

        a2av = a2a_out.ap().rearrange("s (f p) t -> p (s f) t", p=128)

        cc_n = get("g_cc")

        @blk.sync
        def _(sy, cc_n=cc_n):
            sy.wait_ge(g_cc, cc_n)
            sy.dma_start(out=atres[:].rearrange("p (e t) -> p e t", e=c.EC),
                         in_=a2av).then_inc(sy_at[0], 16)
            inc("sy_at0", 16)
        atres_ok = get("sy_at0")

        @blk.gpsimd
        def _(g):
            g.wait_ge(sy_a2, a2a_stores)
            g.dma_start(out=sclf[:], in_=scl.broadcast_to((128, c.D))).then_inc(g_pre, 16)
            inc("g_pre", 16)
        scl_ready = get("g_pre")

        def linear_pass(tag, wsrc, is_i8, resid_src, zdst, stat_streamed, resid_guard,
                        prev_ev=None):
            wv_ = wsrc.rearrange("(e p) d -> p e d", p=128)

            def emit_wload(oc):
                ring = oc % 3

                @blk.gpsimd
                def _(g, oc=oc, ring=ring):
                    if oc >= 3:
                        g.wait_ge(t_all, ct[f"{tag}_mm_oc{oc - 3}"])
                    elif tag == "C":
                        prev = max(a for a in range(c.OCN) if a % 3 == oc % 3)
                        g.wait_ge(t_all, ct["A_mm_oc%d" % prev])
                    else:
                        g.wait_ge(sy_a2, a2a_stores)
                    g.dma_start(out=wring[ring][:].rearrange("p (e d) -> p e d", e=c.EC),
                                in_=wv_[:, :, oc * 512:(oc + 1) * 512]).then_inc(g_w[ring], 16)
                    inc(f"g_w{ring}", 16)
                ct[f"{tag}_wld_oc{oc}"] = (g_w[ring], get(f"g_w{ring}"))

            for oc in range(min(3, c.OCN)):
                emit_wload(oc)

            xq = []
            ct["gv_local"] = 0

            def emit_xload(idx):
                tb = idx % c.OTB
                for oi in range(2):
                    oc = (idx // c.OTB) * 2 + oi
                    gx = get("gx")
                    xq.append(gx)

                    qdec = blk.sync if KNOB['xsync'] else blk.scalar

                    @qdec
                    def _(s, tb=tb, oc=oc, gx=gx):
                        for sm, val in xt_guard.get(gx % NSL, []):
                            s.wait_ge(sm, val)
                        if resid_guard is not None:
                            for sm, val in resid_guard:
                                s.wait_ge(sm, val)
                        s.dma_start(out=xt[gx % NSL][:],
                                    in_=resid_src[tb * 128:(tb + 1) * 128,
                                                  oc * 512:(oc + 1) * 512]).then_inc(scx[gx % NSL], 16)
                        inc(f"scx{gx % NSL}", 16)
                    ct[f"xload{gx}"] = (scx[gx % NSL], get(f"scx{gx % NSL}"))
                    inc("gx")

            emit_xload(0)
            emit_xload(1)

            nidx = c.OCN // 2 * c.OTB
            for idx in range(nidx):
                op_, tb = idx // c.OTB, idx % c.OTB
                oc0, oc1 = 2 * op_, 2 * op_ + 1
                sp = idx % 2

                @blk.tensor
                def _(t, idx=idx, tb=tb, oc0=oc0, oc1=oc1, sp=sp):
                    tw = TW(t)
                    if idx == 0:
                        if stat_streamed:
                            tw.wait(sy_at[0], atres_ok)
                        else:
                            tw.wait(sy_xb, ct["x1T_done"])
                    if tb == 0:
                        for occ in (oc0, oc1):
                            sm, val = ct[f"{tag}_wld_oc{occ}"]
                            tw.wait(sm, val)
                    if idx >= 2:
                        tw.wait(v_all, ct[f"{tag}_ev_idx{idx - 2}"])
                    elif prev_ev is not None:
                        tw.wait(v_all, prev_ev)
                    b0, b1 = (0, 1) if idx % 2 == 0 else (2, 3)
                    r0, r1 = oc0 % 3, oc1 % 3
                    for e in range(c.EC):
                        stb = atres if stat_streamed else x1T
                        lhs = stb[:, e * c.TO + tb * 128:e * c.TO + (tb + 1) * 128]
                        st_, sp_ = (e == 0), (e == c.EC - 1)
                        t.matmul(psb[b0][:], lhs, wring[r0][:, e * 512:(e + 1) * 512],
                                 start=st_, stop=sp_)
                        mm = t.matmul(psb[b1][:], lhs, wring[r1][:, e * 512:(e + 1) * 512],
                                      start=st_, stop=sp_)
                    mm.then_inc(t_all, 1)
                    inc("t_all", 1)
                ct[f"{tag}_mm_idx{idx}"] = get("t_all")
                ct[f"{tag}_mm_oc{oc1}"] = get("t_all")
                ct[f"{tag}_mm_oc{oc0}"] = get("t_all")

                for oi in range(2):
                    oc = oc0 if oi == 0 else oc1
                    gx = xq[get("gv_local")]
                    inc("gv_local")
                    bk = ((0, 1) if idx % 2 == 0 else (2, 3))[oi]

                    @blk.vector
                    def _(v, idx=idx, tb=tb, oc=oc, gx=gx, bk=bk, is_i8=is_i8):
                        vw = VW(v)
                        vw.wait(t_all, ct[f"{tag}_mm_idx{idx}"])
                        vw.wait(ct[f"xload{gx}"][0], ct[f"xload{gx}"][1])
                        for sm, val in zt_guard.get(gx % NSL, []):
                            vw.wait(sm, val)
                        sev = KNOB['sev']
                        if sev == 1:
                            vw.tensor_copy(zt[gx % NSL][:], psb[bk][:])
                        elif sev == 2:
                            vw.tensor_tensor(zt[gx % NSL][:], psb[bk][:], xt[gx % NSL][:], op=ALU.add)
                        else:
                            if is_i8:
                                vw.wait(g_pre, scl_ready)
                                vw.tensor_tensor(zt[gx % NSL][:], psb[bk][:],
                                                 sclf[:, oc * 512:(oc + 1) * 512], op=ALU.mult)
                                vw.tensor_tensor(zt[gx % NSL][:], zt[gx % NSL][:], xt[gx % NSL][:],
                                                 op=ALU.add)
                            else:
                                vw.tensor_tensor(zt[gx % NSL][:], psb[bk][:], xt[gx % NSL][:], op=ALU.add)
                            vw.tensor_reduce(sqcol[:, 0:1], zt[gx % NSL][:], axis=AXX, op=ALU.add)
                            if oc == 0:
                                vw.tensor_copy(zsum[:, tb:tb + 1], sqcol[:, 0:1])
                            else:
                                vw.tensor_tensor(zsum[:, tb:tb + 1], zsum[:, tb:tb + 1],
                                                 sqcol[:, 0:1], op=ALU.add)
                    ct[f"{tag}_evhalf{gx}"] = get("v_all")
                    xt_guard[gx % NSL] = [(v_all, get("v_all"))]

                    if KNOB['sev'] == 0:
                        @blk.vector
                        def _(v, gx=gx, tb=tb, oc=oc):
                            vw = VW(v)
                            vw.tensor_tensor(xt[gx % NSL][:], zt[gx % NSL][:], zt[gx % NSL][:],
                                             op=ALU.mult)
                            vw.tensor_reduce(sqcol[:, 1:2], xt[gx % NSL][:], axis=AXX, op=ALU.add)
                            if oc == 0:
                                vw.tensor_copy(zsq[:, tb:tb + 1], sqcol[:, 1:2])
                            else:
                                vw.tensor_tensor(zsq[:, tb:tb + 1], zsq[:, tb:tb + 1],
                                                 sqcol[:, 1:2], op=ALU.add)
                        xt_guard[gx % NSL].append((v_all, get("v_all")))
                    ct[f"{tag}_sq{gx}"] = get("v_all")


                    @blk.sync
                    def _(sy, tb=tb, oc=oc, gx=gx):
                        sy.wait_ge(v_all, ct[f"{tag}_evhalf{gx}"])
                        sy.dma_start(out=zdst[tb * 128:(tb + 1) * 128, oc * 512:(oc + 1) * 512],
                                     in_=zt[gx % NSL][:]).then_inc(szs[gx % NSL], 16)
                        inc(f"szs{gx % NSL}", 16)
                    zt_guard[gx % NSL] = [(szs[gx % NSL], get(f"szs{gx % NSL}")),
                                          (v_all, get("v_all"))]
                    ct[f"{tag}_ev_idx{idx}"] = get("v_all")

                if idx + 2 < nidx:
                    emit_xload(idx + 2)
                if tb == c.OTB - 1:
                    for onext in (oc1 + 2, oc1 + 3):
                        if onext < c.OCN:
                            emit_wload(onext)
            ct[f"{tag}_z_done"] = [(szs[i], get(f"szs{i}")) for i in range(NSL)]

        def ln_finalize(tag, zsrc, x1_store, do_x1T):
            @blk.vector
            def _(v):
                vw = VW(v)
                vw.tensor_scalar(meanc[:, 0:c.OTB], zsum[:, 0:c.OTB], 1.0 / c.D, None, op0=ALU.mult)
                vw.tensor_scalar(rsig[:, 0:c.OTB], zsq[:, 0:c.OTB], 1.0 / c.D, None, op0=ALU.mult)
                vw.tensor_tensor(tmpc[:, 0:c.OTB], meanc[:, 0:c.OTB], meanc[:, 0:c.OTB], op=ALU.mult)
                vw.tensor_tensor(rsig[:, 0:c.OTB], rsig[:, 0:c.OTB], tmpc[:, 0:c.OTB],
                                 op=ALU.subtract)
                vw.tensor_scalar(rsig[:, 0:c.OTB], rsig[:, 0:c.OTB], 1e-5, None, op0=ALU.add)
            f1 = get("v_all")

            @blk.scalar
            def _(s, f1=f1):
                sw = SW(s)
                sw.wait(v_all, f1)
                sw.activation(rsig[:, 0:c.OTB], rsig[:, 0:c.OTB], AF.Sqrt)
            f2 = get("s_all")

            @blk.vector
            def _(v, f2=f2):
                vw = VW(v)
                vw.wait(s_all, f2)
                vw.reciprocal(rsig[:, 0:c.OTB], rsig[:, 0:c.OTB])
            stats_done = ("v", get("v_all"))

            for tb in range(c.OTB):
                for oc in range(c.OCN):
                    gx = get("gx")

                    @blk.scalar
                    def _(s, tb=tb, oc=oc, gx=gx):
                        for sm, val in xt_guard.get(gx % NSL, []):
                            s.wait_ge(sm, val)
                        for sm, val in ct[f"{tag}_z_done"]:
                            s.wait_ge(sm, val)
                        s.dma_start(out=xt[gx % NSL][:],
                                    in_=zsrc[tb * 128:(tb + 1) * 128,
                                             oc * 512:(oc + 1) * 512]).then_inc(scx[gx % NSL], 16)
                        inc(f"scx{gx % NSL}", 16)
                    ct[f"xload{gx}"] = (scx[gx % NSL], get(f"scx{gx % NSL}"))
                    inc("gx")

                    @blk.vector
                    def _(v, tb=tb, oc=oc, gx=gx, stats_done=stats_done, do_x1T=do_x1T):
                        vw = VW(v)
                        vw.wait(v_all if stats_done[0] == "v" else s_all, stats_done[1])
                        vw.wait(ct[f"xload{gx}"][0], ct[f"xload{gx}"][1])
                        for sm, val in zt_guard.get(gx % NSL, []):
                            vw.wait(sm, val)
                        if do_x1T and f"x1bf_tr{tb - 1}" in ct:
                            vw.wait(sy_xb, ct[f"x1bf_tr{tb - 1}"])
                        vw.tensor_scalar(zt[gx % NSL][:], xt[gx % NSL][:], meanc[:, tb:tb + 1],
                                         rsig[:, tb:tb + 1], op0=ALU.subtract, op1=ALU.mult)
                        if do_x1T:
                            vw.tensor_scalar(x1bf[:, oc * 512:(oc + 1) * 512], xt[gx % NSL][:],
                                             meanc[:, tb:tb + 1], rsig[:, tb:tb + 1],
                                             op0=ALU.subtract, op1=ALU.mult)
                    ct[f"pb_v{gx}"] = get("v_all")
                    xt_guard[gx % NSL] = [(v_all, get("v_all"))]

                    @blk.sync
                    def _(sy, tb=tb, oc=oc, gx=gx):
                        sy.wait_ge(v_all, ct[f"pb_v{gx}"])
                        sy.dma_start(out=x1_store[tb * 128:(tb + 1) * 128,
                                                  oc * 512:(oc + 1) * 512],
                                     in_=zt[gx % NSL][:]).then_inc(szs[gx % NSL], 16)
                        inc(f"szs{gx % NSL}", 16)
                    zt_guard[gx % NSL] = [(szs[gx % NSL], get(f"szs{gx % NSL}"))]

                if do_x1T:
                    last_v = get("v_all")

                    @blk.sync
                    def _(sy, tb=tb, last_v=last_v):
                        sy.wait_ge(v_all, last_v)
                        sy.dma_start_transpose(
                            out=x1T[:].rearrange("p (e f) -> p e f", e=c.EC)
                            [:, :, tb * 128:(tb + 1) * 128],
                            in_=x1bf[:].rearrange("p (e f) -> p e f", e=c.EC),
                        ).then_inc(sy_xb, 16)
                        inc("sy_xb", 16)
                    ct[f"x1bf_tr{tb}"] = get("sy_xb")
            if do_x1T:
                ct["x1T_done"] = get("sy_xb")

        linear_pass("A", woT, False, hs, z1d, True, [(sy_a2, a2a_stores)])
        pa.close()
        x1T = sb(p3, "x1T", [128, c.EC * c.TO], BF16)
        if KNOB['p3max'] >= 2:
            ln_finalize("A", z1d, x1d, True)
        ct["A_x1_done"] = [(szs[i], get(f"szs{i}")) for i in range(NSL)] + \
            [(sy_xb, get("sy_xb"))]
        if KNOB['p3max'] >= 3:
            linear_pass("C", wfT, True, x1d, z2d, False, ct["A_x1_done"],
                        prev_ev=ct["A_ev_idx%d" % (c.OCN // 2 * c.OTB - 1)])
        if KNOB['p3max'] >= 4:
            ln_finalize("C", z2d, out, False)

        @blk.sync
        def _(sy):
            for i in range(NSL):
                sy.wait_ge(szs[i], get(f"szs{i}"))
            sy.wait_ge(sy_xb, get("sy_xb"))
            if debug:
                sy.wait_ge(v_all, get("v_all"))
                sy.dma_start(out=dledd, in_=dled[:]).then_inc(sy_km, 16)
                inc("sy_km", 16)
                sy.dma_start(out=a2ad, in_=a2a_in.ap()).then_inc(sy_km, 16)
                inc("sy_km", 16)
                sy.wait_ge(sy_km, get("sy_km"))

        p3.close()

    return nc, ct


def _prep_inputs(c, hsf, wq, wk, wv, woTm, wfTm, sclm):
    pos = np.arange(c.S, dtype=np.float64)
    inv = 1.0 / (10000.0 ** (np.arange(0, c.ROT, 2, dtype=np.float64) / c.ROT))
    ang = pos[:, None] * inv[None, :]
    cosdm = np.ascontiguousarray(np.tile(np.cos(ang).astype(np.float32), (1, c.HPC)))
    sindm = np.ascontiguousarray(np.tile(np.sin(ang).astype(np.float32), (1, c.HPC)))

    nm = c.KC // 128
    m4 = np.zeros((nm, 128, c.KC), dtype=np.float32)
    r = np.arange(128)[:, None]
    cc = np.arange(c.KC)[None, :]
    for j in range(nm):
        m4[j] = np.where(cc <= j * 128 + r, 0.0, -1e9)
    m4 = m4.astype(ml_dtypes.bfloat16)

    maps = []
    for core in range(NCORES):
        maps.append({
            "hs": np.ascontiguousarray(hsf[core * c.TO:(core + 1) * c.TO]),
            "hsf": hsf,
            "wqT": np.ascontiguousarray(wq[core * c.DH:(core + 1) * c.DH].T),
            "wkT": np.ascontiguousarray(wk[core * c.DH:(core + 1) * c.DH].T),
            "wvT": np.ascontiguousarray(wv[core * c.DH:(core + 1) * c.DH].T),
            "woT": woTm, "wfT": wfTm, "scl": sclm,
            "cosd": cosdm, "sind": sindm, "m4d": m4,
        })
    return maps


def kernel(L0_attn_hidden_states, L0_attn_w_q, L0_attn_w_k, L0_attn_w_v,
           L0_attn_w_out, L0_ffn_weight, L0_ffn_scales):
    c = FULL
    hsf = np.asarray(L0_attn_hidden_states, dtype=np.float32).reshape(c.T, c.D)
    wq = np.asarray(L0_attn_w_q, dtype=np.float32)
    wk = np.asarray(L0_attn_w_k, dtype=np.float32)
    wv = np.asarray(L0_attn_w_v, dtype=np.float32)
    woTm = np.ascontiguousarray(np.asarray(L0_attn_w_out, dtype=np.float32).T)
    wfTm = np.ascontiguousarray(np.asarray(L0_ffn_weight).T)
    sclm = np.ascontiguousarray(np.asarray(L0_ffn_scales, dtype=np.float32).reshape(1, c.D))

    nc, _ = build_program(c)
    maps = _prep_inputs(c, hsf, wq, wk, wv, woTm, wfTm, sclm)
    res = run_bass_kernel_spmd(nc, maps, core_ids=list(range(NCORES)))
    full = np.concatenate([np.asarray(res.results[i]["out"]) for i in range(NCORES)], axis=0)
    return full.reshape(c.B, c.S, c.D)


# version 16 (final)
# speedup vs baseline: 1.0213x; 1.0213x over previous
import contextlib
import math
import numpy as np
import concourse.bass as bass
import concourse.mybir as mybir
from concourse.bass_utils import run_bass_kernel_spmd
from concourse.aot_env import allow_ici_all_to_all
import ml_dtypes

F32 = mybir.dt.float32
BF16 = mybir.dt.bfloat16
I8 = mybir.dt.int8
AF = mybir.ActivationFunctionType
ALU = mybir.AluOpType
AXX = mybir.AxisListType.X

NCORES = 8


class Cfg:
    def __init__(self, B=4, S=2048, D=4096, HEADS=16, HD=256, ROT=64):
        self.B, self.S, self.D, self.HEADS, self.HD, self.ROT = B, S, D, HEADS, HD, ROT
        self.T = B * S
        self.TO = self.T // NCORES
        self.HPC = HEADS // NCORES
        self.DH = self.HPC * HD
        self.EC = D // 128
        self.TBN = self.T // 128
        self.UB = S // 128
        self.KC = min(512, S)
        self.QC = min(512, S)
        self.NKC = S // self.KC
        self.NQC = S // self.QC
        self.DC = HD // 128
        self.OCN = D // 512
        self.OTB = self.TO // 128
        self.NF = ROT // 2
        self.TC = min(512, self.TO)
        self.NCH = self.TO // self.TC
        self.NTC = self.T // self.TC
        self.NUN = self.B * self.HPC


FULL = Cfg()
KNOB = {'rope': True, 'norms': True, 'spill': True, 'kmax': True, 'qkv': True, 'maxtc': 10**9, 'ropebanks': 2, 'nop0': False, 'nosq': False, 'nokb': False, 'p3max': 4, 'xsync': False, 'noat': False, 'sev': 0}


class EW:

    def __init__(self, eng, semh, ct, key):
        self.e, self.s, self.ct, self.key = eng, semh, ct, key

    def n(self):
        return self.ct.get(self.key, 0)

    def wait(self, semh, val):
        if val > 0:
            self.e.wait_ge(semh, val)

    def __getattr__(self, name):
        f = getattr(self.e, name)
        ew = self

        def call(*a, **k):
            if ew.n() > 0:
                ew.e.wait_ge(ew.s, ew.n())
            instr = f(*a, **k)
            instr.then_inc(ew.s, 1)
            ew.ct[ew.key] = ew.n() + 1
            return instr
        return call


def build_program(c, debug=False, phases=3):
    nc = bass.Bass("TRN2", target_bir_lowering=False)
    ct = {}

    def inc(key, n=1):
        ct[key] = ct.get(key, 0) + n
        return ct[key]

    def get(key):
        return ct.get(key, 0)

    okind = "ExternalOutput" if debug else "Internal"

    hs = nc.dram_tensor("hs", [c.TO, c.D], F32, kind="ExternalInput").ap()
    hsf = nc.dram_tensor("hsf", [c.T, c.D], F32, kind="ExternalInput").ap()
    wqT = nc.dram_tensor("wqT", [c.D, c.DH], F32, kind="ExternalInput").ap()
    wkT = nc.dram_tensor("wkT", [c.D, c.DH], F32, kind="ExternalInput").ap()
    wvT = nc.dram_tensor("wvT", [c.D, c.DH], F32, kind="ExternalInput").ap()
    woT = nc.dram_tensor("woT", [c.D, c.D], F32, kind="ExternalInput").ap()
    wfT = nc.dram_tensor("wfT", [c.D, c.D], I8, kind="ExternalInput").ap()
    scl = nc.dram_tensor("scl", [1, c.D], F32, kind="ExternalInput").ap()
    cosd = nc.dram_tensor("cosd", [c.S, c.HPC * c.NF], F32, kind="ExternalInput").ap()
    sind = nc.dram_tensor("sind", [c.S, c.HPC * c.NF], F32, kind="ExternalInput").ap()
    m4d = nc.dram_tensor("m4d", [c.KC // 128, 128, c.KC], BF16, kind="ExternalInput").ap()
    out = nc.dram_tensor("out", [c.TO, c.D], F32, kind="ExternalOutput").ap()

    Qn = nc.dram_tensor("Qn", [c.T, c.DH], BF16, kind=okind)
    Kn = nc.dram_tensor("Kn", [c.T, c.DH], BF16, kind=okind)
    Vn = nc.dram_tensor("Vn", [c.T, c.DH], BF16, kind=okind)
    a2a_in = nc.dram_tensor("a2a_in", [NCORES, c.DH, c.TO], BF16)
    a2a_out = nc.dram_tensor("a2a_out", [NCORES, c.DH, c.TO], BF16)
    z1d = nc.dram_tensor("z1d", [c.TO, c.D], F32, kind=okind)
    x1d = nc.dram_tensor("x1d", [c.TO, c.D], F32, kind=okind)
    z2d = nc.dram_tensor("z2d", [c.TO, c.D], F32)
    if debug:
        a2ad = nc.dram_tensor("a2ad", [NCORES, c.DH, c.TO], BF16, kind="ExternalOutput").ap()
        a2apre = nc.dram_tensor("a2apre", [NCORES, c.DH, c.TO], BF16, kind="ExternalOutput").ap()
        dledd = nc.dram_tensor("dledd", [128, 3 * c.NUN * c.UB], F32, kind="ExternalOutput").ap()

    with contextlib.ExitStack() as st:
        def sb(stk, name, shape, dt):
            return stk.enter_context(nc.sbuf_tensor(name, shape, dt))

        def sem(name):
            return st.enter_context(nc.semaphore(name))

        blk = st.enter_context(nc.Block())

        junk = sb(st, "junk", [128, 264], F32)
        zsum = sb(st, "zsum", [128, c.OTB], F32)
        zsq = sb(st, "zsq", [128, c.OTB], F32)
        meanc = sb(st, "meanc", [128, c.OTB], F32)
        rsig = sb(st, "rsig", [128, c.OTB], F32)
        tmpc = sb(st, "tmpc", [128, c.OTB], F32)
        sqcol = sb(st, "sqcol", [128, 2], F32)
        dled = sb(st, "dled", [128, 3 * c.NUN * c.UB], F32) if debug else None

        psb = [st.enter_context(nc.psum_tensor(f"pb{i}", [128, 512], F32)) for i in range(8)]

        t_all = sem("t_all")
        v_all = sem("v_all")
        s_all = sem("s_all")
        g_pre = sem("g_pre")
        g_cc = sem("g_cc")
        g_w = [sem(f"g_w{i}") for i in range(3)]
        sy_p0h = [sem("sy_p0a"), sem("sy_p0b")]
        sy_h = [sem(f"sy_h{i}") for i in range(2)]
        sy_sp = sem("sy_sp")
        sy_km = sem("sy_km")
        g_hs = [sem("g_hs0"), sem("g_hs1")]
        sy_u = [sem(f"sy_u{i}") for i in range(2)]
        sy_tr = [sem(f"sy_tr{i}") for i in range(2)]
        sy_pt = [sem(f"sy_pt{i}") for i in range(2)]
        sy_a2 = sem("sy_a2")
        sy_at = [sem(f"sy_at{i}") for i in range(2)]
        NSL = 4
        scx = [sem(f"scx{i}") for i in range(NSL)]
        szs = [sem(f"szs{i}") for i in range(NSL)]
        sy_xb = sem("sy_xb")

        def TW(t):
            return EW(t, t_all, ct, "t_all")

        def VW(v):
            return EW(v, v_all, ct, "v_all")

        def SW(s):
            return EW(s, s_all, ct, "s_all")

        xt_guard = {}
        zt_guard = {}

        p1 = contextlib.ExitStack()
        cosT = sb(p1, "cosT", [128, c.UB * c.HPC * c.NF], F32)
        sinT = sb(p1, "sinT", [128, c.UB * c.HPC * c.NF], F32)
        wq_sb = sb(p1, "wq_sb", [128, c.EC * c.DH], BF16)
        wk_sb = sb(p1, "wk_sb", [128, c.EC * c.DH], BF16)
        wv_sb = sb(p1, "wv_sb", [128, c.EC * c.DH], BF16)
        hst = [sb(p1, f"hst{i}", [128, c.EC * c.TC], BF16) for i in range(2)]
        natb = sb(p1, "natb", [128, 2 * c.D], BF16)
        qstg = sb(p1, "qstg", [128, 4 * c.DH], BF16)
        kstg = sb(p1, "kstg", [128, 4 * c.DH], BF16)
        vstg = sb(p1, "vstg", [128, 4 * c.DH], BF16)
        rt = sb(p1, "rt", [128, 4 * c.HPC * c.NF], F32)

        if debug:
            @blk.vector
            def _(v):
                vw = VW(v)
                vw.memset(dled[:], 0.0)

        @blk.gpsimd
        def _(g):
            for wsrc, wdst in ((wqT, wq_sb), (wkT, wk_sb), (wvT, wv_sb)):
                g.dma_start(out=wdst[:].rearrange("p (e d) -> p e d", e=c.EC),
                            in_=wsrc.rearrange("(e p) d -> p e d", p=128)).then_inc(g_pre, 16)
                inc("g_pre", 16)
            g.dma_start(out=cosT[:].rearrange("p (a f) -> p a f", a=c.UB),
                        in_=cosd.rearrange("(a p) f -> p a f", p=128)).then_inc(g_pre, 16)
            inc("g_pre", 16)
            g.dma_start(out=sinT[:].rearrange("p (a f) -> p a f", a=c.UB),
                        in_=sind.rearrange("(a p) f -> p a f", p=128)).then_inc(g_pre, 16)
            inc("g_pre", 16)
        prep_done = get("g_pre")

        if phases < 1:
            @blk.sync
            def _(sy):
                sy.wait_ge(g_cc, get("g_cc"))
            p1.close()
            return nc, ct

        tbpc = c.TC // 128

        def emit_hst_load(tc):
            slot = tc % 2
            nblk = c.TC // 128
            for s in range(nblk):
                b = s % 2

                @blk.gpsimd
                def _(g, tc=tc, s=s, b=b):
                    guard = ct.get(f"natb_tr{b}")
                    if guard is not None:
                        g.wait_ge(sy_p0h[b], guard * 16)
                    g.dma_start(
                        out=natb[:].rearrange("p (b2 e) -> p b2 e", b2=2)[:, b],
                        in_=hsf[tc * c.TC + s * 128:tc * c.TC + (s + 1) * 128, :],
                    ).then_inc(g_hs[b], 16)
                    inc(f"g_hs{b}", 16)
                cast_done = get(f"g_hs{b}")

                @blk.sync
                def _(sy, tc=tc, slot=slot, s=s, b=b, cast_done=cast_done):
                    sy.wait_ge(g_hs[b], cast_done)
                    if s == 0 and tc >= 2:
                        sy.wait_ge(t_all, ct[f"qkv_tc{tc - 2}"])
                    sy.dma_start_transpose(
                        out=hst[slot][:].rearrange("p (e a f) -> p e a f", e=c.EC,
                                                   a=nblk)[:, :, s, :],
                        in_=natb[:].rearrange("p (b2 e f) -> p b2 e f", b2=2,
                                              e=c.EC)[:, b],
                    ).then_inc(sy_p0h[b], 16)
                    inc(f"sy_p0{b}")
                ct[f"natb_tr{b}"] = get(f"sy_p0{b}")
            ct[f"hst_tc{tc}"] = [(sy_p0h[hh], ct.get(f"natb_tr{hh}", 0) * 16)
                                 for hh in range(min(2, nblk))]

        emit_hst_load(0)
        if c.NTC > 1:
            emit_hst_load(1)

        wview = [wq_sb[:].rearrange("p (e d) -> p e d", e=c.EC),
                 wk_sb[:].rearrange("p (e d) -> p e d", e=c.EC),
                 wv_sb[:].rearrange("p (e d) -> p e d", e=c.EC)]

        for tc in range(min(c.NTC, KNOB['maxtc'])):
            slot = tc % 2
            for j in range(tbpc):
                gtb = tc * tbpc + j
                bq, bk, bv = (0, 1, 2) if gtb % 2 == 0 else (3, 4, 5)
                stg = gtb % 4

                @blk.tensor
                def _(t, tc=tc, slot=slot, j=j, gtb=gtb, bq=bq, bk=bk, bv=bv):
                    tw = TW(t)
                    if gtb == 0:
                        tw.wait(g_pre, prep_done)
                    if j == 0:
                        for sm, val in ct[f"hst_tc{tc}"]:
                            tw.wait(sm, val)
                    if gtb >= 2:
                        tw.wait(v_all, ct[f"ev_v_tb{gtb - 2}"])
                    hv = hst[slot][:].rearrange("p (e t) -> p e t", e=c.EC)
                    for e in range(c.EC):
                        lhs = hv[:, e, j * 128:(j + 1) * 128]
                        st_, sp_ = (e == 0), (e == c.EC - 1)
                        t.matmul(psb[bq][:, 0:c.DH], lhs, wview[0][:, e, :], start=st_, stop=sp_)
                        t.matmul(psb[bk][:, 0:c.DH], lhs, wview[1][:, e, :], start=st_, stop=sp_)
                        mm = t.matmul(psb[bv][:, 0:c.DH], lhs, wview[2][:, e, :], start=st_, stop=sp_)
                    mm.then_inc(t_all, 1)
                    inc("t_all", 1)
                ct[f"qkv_tb{gtb}"] = get("t_all")
                if j == tbpc - 1:
                    ct[f"qkv_tc{tc}"] = get("t_all")

                @blk.vector
                def _(v, gtb=gtb, bq=bq, bk=bk, bv=bv, stg=stg):
                    vw = VW(v)
                    vw.wait(t_all, ct[f"qkv_tb{gtb}"])
                    grp = gtb // 4
                    if grp >= 1 and stg == 0 and KNOB['spill']:
                        vw.wait(sy_sp, 48 * grp)
                    ub = gtb % c.UB
                    ctab = cosT[:].rearrange("p (a h f) -> p a h f", a=c.UB, h=c.HPC)[:, ub]
                    stab = sinT[:].rearrange("p (a h f) -> p a h f", a=c.UB, h=c.HPC)[:, ub]
                    rv = rt[:].rearrange("p (i h f) -> p i h f", i=4, h=c.HPC)
                    for bi, (bank, outstg) in enumerate(((bq, qstg), (bk, kstg))):
                        pv = psb[bank][:, 0:c.DH].rearrange("p (h d) -> p h d", h=c.HPC)
                        ov = outstg[:].rearrange("p (s h d) -> p s h d", s=4, h=c.HPC)[:, stg]
                        if KNOB['rope'] and bi < KNOB['ropebanks']:
                            lvl = 3 if KNOB['rope'] is True else KNOB['rope']
                            ev, od = pv[:, :, 0:c.ROT:2], pv[:, :, 1:c.ROT:2]
                            oev, ood = ov[:, :, 0:c.ROT:2], ov[:, :, 1:c.ROT:2]
                            sel = KNOB.get('ropesel', [0, 1, 2, 3])
                            ops = [(0, ev, ctab), (1, od, stab), (2, od, ctab), (3, ev, stab)]
                            for ii, aa, bb in ops:
                                if ii in sel and (ii == 0 or lvl >= 2):
                                    vw.tensor_tensor(rv[:, ii], aa, bb, op=ALU.mult)
                            if lvl >= 3:
                                vw.tensor_tensor(oev, rv[:, 0], rv[:, 1], op=ALU.subtract)
                                vw.tensor_tensor(ood, rv[:, 2], rv[:, 3], op=ALU.add)
                            vw.tensor_copy(ov[:, :, 0:c.ROT] if lvl < 3 else ov[:, :, c.ROT:],
                                           pv[:, :, 0:c.ROT] if lvl < 3 else pv[:, :, c.ROT:])
                            if lvl < 3 and c.HD > c.ROT:
                                vw.tensor_copy(ov[:, :, c.ROT:], pv[:, :, c.ROT:])
                        else:
                            vw.tensor_copy(outstg[:, stg * c.DH:(stg + 1) * c.DH],
                                           psb[bank][:, 0:c.DH])
                    vw.tensor_copy(vstg[:, stg * c.DH:(stg + 1) * c.DH], psb[bv][:, 0:c.DH])
                ct[f"ev_v_tb{gtb}"] = get("v_all")

                if stg == 3 and KNOB['spill']:
                    grp = gtb // 4

                    @blk.sync
                    def _(sy, grp=grp, gtb=gtb):
                        sy.wait_ge(v_all, ct[f"ev_v_tb{gtb}"])
                        for stg_t, hbm_t in ((qstg, Qn), (kstg, Kn), (vstg, Vn)):
                            sy.dma_start(
                                out=hbm_t[grp * 512:(grp + 1) * 512, :]
                                .rearrange("(a p) d -> p a d", p=128),
                                in_=stg_t[:].rearrange("p (a d) -> p a d", a=4)
                            ).then_inc(sy_sp, 16)
                            inc("sy_sp", 16)
                if tc + 2 < c.NTC and j == tbpc - 1:
                    emit_hst_load(tc + 2)
        p1_spills = get("sy_sp")

        p1.close()
        p2 = contextlib.ExitStack()
        m4 = sb(p2, "m4", [128, (c.KC // 128) * c.KC], BF16)
        qT = [sb(p2, f"qT{i}", [128, c.DC * c.S], BF16) for i in range(2)]
        kT = [sb(p2, f"kT{i}", [128, c.DC * c.S], BF16) for i in range(2)]
        vU = [sb(p2, f"vU{i}", [128, c.UB * c.HD], BF16) for i in range(2)]
        nstg = [sb(p2, f"nstg{i}", [128, c.UB * c.HD], BF16) for i in range(2)]
        pnat = [sb(p2, f"pnat{i}", [128, c.S], BF16) for i in range(2)]
        sst = [sb(p2, f"sst{i}", [128, c.S], F32) for i in range(2)]
        pT = [sb(p2, f"pT{i}", [128, (c.S // 128) * c.QC], BF16) for i in range(2)]
        avst = sb(p2, "avst", [128, c.DC * c.QC], BF16)
        lsum = [sb(p2, f"lsum{i}", [128, 4 + c.NKC], F32) for i in range(2)]

        @blk.gpsimd
        def _(g):
            g.wait_ge(sy_sp, p1_spills)
            g.dma_start(out=m4[:].rearrange("p (a k) -> p a k", a=c.KC // 128),
                        in_=m4d.rearrange("a p k -> p a k")).then_inc(g_pre, 16)
            inc("g_pre", 16)
        m4_ready = get("g_pre")

        for u in range(c.NUN):
            b, h = u // c.HPC, u % c.HPC
            par = u % 2
            su, str_, sun, strn = sy_u[par], sy_tr[par], f"sy_u{par}", f"sy_tr{par}"

            @blk.sync
            def _(sy, b=b, h=h, par=par, su=su, str_=str_, sun=sun, strn=strn, u=u):
                sy.wait_ge(sy_sp, p1_spills)
                if u >= 2:
                    sy.wait_ge(t_all, ct[f"unit_t{u - 2}"])
                for src_t, dst in ((Kn, kT), (Qn, qT)):
                    for dc_ in range(c.DC):
                        sy.dma_start(
                            out=nstg[par][:].rearrange("p (dc a f) -> p dc a f",
                                                       dc=c.DC, a=c.UB)[:, dc_],
                            in_=src_t[b * c.S:(b + 1) * c.S,
                                      h * c.HD + dc_ * 128:h * c.HD + (dc_ + 1) * 128]
                            .rearrange("(a p) f -> p a f", p=128)).then_inc(su, 16)
                        inc(sun, 16)
                    sy.wait_ge(su, get(sun))
                    for dc_ in range(c.DC):
                        sy.dma_start_transpose(
                            out=dst[par][:].rearrange("p (dc a f) -> p dc a f", dc=c.DC,
                                                      a=c.UB)[:, dc_],
                            in_=nstg[par][:].rearrange("p (dc a f) -> p dc a f", dc=c.DC,
                                                       a=c.UB)[:, dc_],
                        ).then_inc(str_, 16)
                        inc(strn, 16)
                    sy.wait_ge(str_, get(strn))
                sy.dma_start(out=vU[par][:],
                             in_=Vn[b * c.S:(b + 1) * c.S, h * c.HD:(h + 1) * c.HD]
                             .rearrange("(a p) d -> p a d", p=128)).then_inc(su, 16)
                inc(sun, 16)
            u_loaded = get(sun)
            tr_loaded = get(strn)

            for qc in range(c.NQC):
                gch = get("gch")
                pcp = gch % 2
                vis = qc + 1
                pend = ct.get("pend_back")
                for jj in range(c.QC // 128):
                    gqb = inc("gqb")
                    qb = qc * (c.QC // 128) + jj
                    pp = (gqb - 1) % 2

                    def emit_score_chunk(kc, pp, qb, par, bank, sst_free):
                        bank_guard = ct.get(f"sbank_free{bank}")

                        @blk.tensor
                        def _(t, qb=qb, kc=kc, bank=bank, par=par, bank_guard=bank_guard):
                            tw = TW(t)
                            if bank_guard is not None:
                                tw.wait(bank_guard[0], bank_guard[1])
                            if kc == 0:
                                tw.wait(str_, tr_loaded)
                            for dc in range(c.DC):
                                mm = t.matmul(
                                    psb[bank][:, 0:c.KC],
                                    qT[par][:, dc * c.S + qb * 128:dc * c.S + (qb + 1) * 128],
                                    kT[par][:, dc * c.S + kc * c.KC:dc * c.S + (kc + 1) * c.KC],
                                    start=(dc == 0), stop=(dc == c.DC - 1))
                            mm.then_inc(t_all, 1)
                            inc("t_all", 1)
                        t_s_here = get("t_all")

                        @blk.vector
                        def _(v, kc=kc, bank=bank, pp=pp, t_s_here=t_s_here, sst_free=sst_free):
                            vw = VW(v)
                            vw.wait(t_all, t_s_here)
                            if kc == 0 and sst_free is not None:
                                vw.wait(s_all, sst_free)
                            vw.tensor_copy(sst[pp][:, kc * c.KC:(kc + 1) * c.KC],
                                           psb[bank][:, 0:c.KC])
                        ct[f"sbank_free{bank}"] = (v_all, get("v_all"))

                    SBANKS = (6, 7, 0, 1)
                    sst_free = ct.get(f"sst_exp_pp{pp}")
                    for kc in range(vis):
                        emit_score_chunk(kc, pp, qb, par, SBANKS[(inc("sbank") - 1) % 4],
                                         sst_free if kc == 0 else None)
                    @blk.vector
                    def _(v, pp=pp, vis=vis, qb=qb):
                        vw = VW(v)
                        vw.wait(g_pre, m4_ready)
                        mi = qb % (c.KC // 128)
                        vw.tensor_tensor(sst[pp][:, (vis - 1) * c.KC:vis * c.KC],
                                         sst[pp][:, (vis - 1) * c.KC:vis * c.KC],
                                         m4[:, mi * c.KC:(mi + 1) * c.KC], op=ALU.add)
                        vw.tensor_reduce(lsum[pp][:, 0:1], sst[pp][:, 0:vis * c.KC],
                                         axis=AXX, op=ALU.max)
                        vw.tensor_scalar(lsum[pp][:, 1:2], lsum[pp][:, 0:1],
                                         -1.0 / math.sqrt(c.HD), None, op0=ALU.mult)
                    bias_ok = get("v_all")

                    def make_back(gqb=gqb, pp=pp, pcp=pcp, jj=jj, vis=vis, gch=gch,
                                  bias_ok=bias_ok):
                        def back():
                            pn_guard = ct[f"pt_strip{gqb - 2}"] if gqb > 2 else None

                            @blk.scalar
                            def _(s, pp=pp, vis=vis, bias_ok=bias_ok, pn_guard=pn_guard):
                                sw = SW(s)
                                sw.wait(v_all, bias_ok)
                                if pn_guard is not None:
                                    sw.wait(sy_pt[pp], pn_guard)
                                sw.activation(pnat[pp][:, 0:vis * c.KC],
                                              sst[pp][:, 0:vis * c.KC],
                                              AF.Exp, bias=lsum[pp][:, 1:2],
                                              scale=1.0 / math.sqrt(c.HD))
                            ct[f"sst_exp_pp{pp}"] = get("s_all")
                            strip_exp = get("s_all")

                            @blk.vector
                            def _(v, pp=pp, vis=vis, strip_exp=strip_exp, gqb=gqb):
                                vw = VW(v)
                                vw.wait(s_all, strip_exp)
                                vw.tensor_reduce(lsum[pp][:, 2:3], pnat[pp][:, 0:vis * c.KC],
                                                 axis=AXX, op=ALU.add)
                                vw.reciprocal(lsum[pp][:, 3:4], lsum[pp][:, 2:3])
                                if debug:
                                    vw.tensor_copy(dled[:, (gqb - 1) * 2:(gqb - 1) * 2 + 2],
                                                   lsum[pp][:, 2:4])
                                vw.tensor_scalar(pnat[pp][:, 0:vis * c.KC],
                                                 pnat[pp][:, 0:vis * c.KC],
                                                 lsum[pp][:, 3:4], None, op0=ALU.mult)
                            scale_v = get("v_all")

                            @blk.sync
                            def _(sy, pp=pp, pcp=pcp, jj=jj, vis=vis, gch=gch, scale_v=scale_v):
                                sy.wait_ge(v_all, scale_v)
                                if jj == 0 and gch >= 2:
                                    sy.wait_ge(t_all, ct[f"av_t_ch{gch - 2}"])
                                nb = vis * (c.KC // 128)
                                sy.dma_start_transpose(
                                    out=pT[pcp][:].rearrange("p (a f) -> p a f", f=c.QC)
                                    [:, 0:nb, jj * 128:(jj + 1) * 128],
                                    in_=pnat[pp][:, 0:vis * c.KC].rearrange("p (a f) -> p a f",
                                                                            a=nb),
                                ).then_inc(sy_pt[pp], 16)
                                inc(f"sy_pt{pp}", 16)
                            ct[f"pt_strip{gqb}"] = get(f"sy_pt{pp}")
                        return back

                    if pend is not None:
                        pend()
                    pend = make_back()
                if pend is not None:
                    pend()
                    pend = None
                ct["pend_back"] = None

                nkb = vis * (c.KC // 128)
                pt0_done, pt1_done = get("sy_pt0"), get("sy_pt1")

                @blk.tensor
                def _(t, gch=gch, pcp=pcp, par=par, nkb=nkb, pt0=pt0_done, pt1=pt1_done,
                      u_loaded=u_loaded, su=su):
                    tw = TW(t)
                    tw.wait(sy_pt[0], pt0)
                    tw.wait(sy_pt[1], pt1)
                    tw.wait(su, u_loaded)
                    if gch >= 2:
                        tw.wait(v_all, ct[f"av_v_ch{gch - 2}"])
                    ab = 2 if gch % 2 == 0 else 4
                    for dc in range(c.DC):
                        for kb in range(nkb):
                            mm = t.matmul(
                                psb[ab + dc][:, 0:c.QC],
                                vU[par][:, kb * c.HD + dc * 128:kb * c.HD + (dc + 1) * 128],
                                pT[pcp][:, kb * c.QC:(kb + 1) * c.QC],
                                start=(kb == 0), stop=(kb == nkb - 1))
                        mm.then_inc(t_all, 1)
                        inc("t_all", 1)
                ct[f"av_t_ch{gch}"] = get("t_all")

                @blk.vector
                def _(v, gch=gch):
                    vw = VW(v)
                    vw.wait(t_all, ct[f"av_t_ch{gch}"])
                    if gch >= 1:
                        vw.wait(sy_a2, ct[f"a2st_ch{gch - 1}"])
                    ab = 2 if gch % 2 == 0 else 4
                    for dc in range(c.DC):
                        vw.tensor_copy(avst[:, dc * c.QC:(dc + 1) * c.QC], psb[ab + dc][:, 0:c.QC])
                ct[f"av_v_ch{gch}"] = get("v_all")

                @blk.sync
                def _(sy, b=b, h=h, qc=qc, gch=gch):
                    sy.wait_ge(v_all, ct[f"av_v_ch{gch}"])
                    qg = b * c.S + qc * c.QC
                    slot, col = qg // c.TO, qg % c.TO
                    for dc in range(c.DC):
                        sy.dma_start(
                            out=a2a_in[slot, h * c.HD + dc * 128:h * c.HD + (dc + 1) * 128,
                                       col:col + c.QC],
                            in_=avst[:, dc * c.QC:(dc + 1) * c.QC]).then_inc(sy_a2, 16)
                        inc("sy_a2", 16)
                ct[f"a2st_ch{gch}"] = get("sy_a2")
                inc("gch")

            ct[f"unit_t{u}"] = get("t_all")
            ct[f"unit_s{u}"] = get("s_all")

        a2a_stores = get("sy_a2")

        if phases < 3:
            @blk.sync
            def _(sy):
                sy.wait_ge(sy_a2, a2a_stores)
            p2.close()
            return nc, ct

        if debug:
            @blk.sync
            def _(sy):
                sy.wait_ge(sy_a2, a2a_stores)
                sy.dma_start(out=a2apre, in_=a2a_in.ap()).then_inc(sy_km, 16)
                inc("sy_km", 16)

        @blk.gpsimd
        def _(g):
            g.wait_ge(sy_a2, a2a_stores)
            if debug:
                g.wait_ge(sy_km, get("sy_km"))
            with allow_ici_all_to_all():
                g.collective_compute(
                    "AllToAll", ALU.bypass, replica_groups=[list(range(NCORES))],
                    ins=[a2a_in.ap().opt()], outs=[a2a_out.ap().opt()],
                ).then_inc(g_cc, 1)
            inc("g_cc", 1)

        p2.close()
        p3 = contextlib.ExitStack()
        wring = [sb(p3, f"wring{i}", [128, c.EC * 512], BF16) for i in range(3)]
        x1bf = sb(p3, "x1bf", [128, c.D], BF16)
        sclf = sb(p3, "sclf", [128, c.D], BF16)
        xt = [sb(p3, f"xt{i}", [128, 512], F32) for i in range(NSL)]
        zt = [sb(p3, f"zt{i}", [128, 512], F32) for i in range(NSL)]
        pa = contextlib.ExitStack()
        atres = sb(pa, "atres", [128, c.EC * c.TO], BF16)
        x1T = None

        a2av = a2a_out.ap().rearrange("s (f p) t -> p (s f) t", p=128)

        cc_n = get("g_cc")

        @blk.sync
        def _(sy, cc_n=cc_n):
            sy.wait_ge(g_cc, cc_n)
            sy.dma_start(out=atres[:].rearrange("p (e t) -> p e t", e=c.EC),
                         in_=a2av).then_inc(sy_at[0], 16)
            inc("sy_at0", 16)
        atres_ok = get("sy_at0")

        @blk.gpsimd
        def _(g):
            g.wait_ge(sy_a2, a2a_stores)
            g.dma_start(out=sclf[:], in_=scl.broadcast_to((128, c.D))).then_inc(g_pre, 16)
            inc("g_pre", 16)
        scl_ready = get("g_pre")

        def linear_pass(tag, wsrc, is_i8, resid_src, zdst, stat_streamed, resid_guard,
                        prev_ev=None):
            wv_ = wsrc.rearrange("(e p) d -> p e d", p=128)

            def emit_wload(oc):
                ring = oc % 3

                @blk.gpsimd
                def _(g, oc=oc, ring=ring):
                    if oc >= 3:
                        g.wait_ge(t_all, ct[f"{tag}_mm_oc{oc - 3}"])
                    elif tag == "C":
                        prev = max(a for a in range(c.OCN) if a % 3 == oc % 3)
                        g.wait_ge(t_all, ct["A_mm_oc%d" % prev])
                    else:
                        g.wait_ge(sy_a2, a2a_stores)
                    g.dma_start(out=wring[ring][:].rearrange("p (e d) -> p e d", e=c.EC),
                                in_=wv_[:, :, oc * 512:(oc + 1) * 512]).then_inc(g_w[ring], 16)
                    inc(f"g_w{ring}", 16)
                ct[f"{tag}_wld_oc{oc}"] = (g_w[ring], get(f"g_w{ring}"))

            for oc in range(min(3, c.OCN)):
                emit_wload(oc)

            xq = []
            ct["gv_local"] = 0

            def emit_xload(idx):
                tb = idx % c.OTB
                for oi in range(2):
                    oc = (idx // c.OTB) * 2 + oi
                    gx = get("gx")
                    xq.append(gx)

                    qdec = blk.sync if KNOB['xsync'] else blk.scalar

                    @qdec
                    def _(s, tb=tb, oc=oc, gx=gx):
                        for sm, val in xt_guard.get(gx % NSL, []):
                            s.wait_ge(sm, val)
                        if resid_guard is not None:
                            for sm, val in resid_guard:
                                s.wait_ge(sm, val)
                        s.dma_start(out=xt[gx % NSL][:],
                                    in_=resid_src[tb * 128:(tb + 1) * 128,
                                                  oc * 512:(oc + 1) * 512]).then_inc(scx[gx % NSL], 16)
                        inc(f"scx{gx % NSL}", 16)
                    ct[f"xload{gx}"] = (scx[gx % NSL], get(f"scx{gx % NSL}"))
                    inc("gx")

            emit_xload(0)
            emit_xload(1)

            nidx = c.OCN // 2 * c.OTB
            for idx in range(nidx):
                op_, tb = idx // c.OTB, idx % c.OTB
                oc0, oc1 = 2 * op_, 2 * op_ + 1
                sp = idx % 2

                @blk.tensor
                def _(t, idx=idx, tb=tb, oc0=oc0, oc1=oc1, sp=sp):
                    tw = TW(t)
                    if idx == 0:
                        if stat_streamed:
                            tw.wait(sy_at[0], atres_ok)
                        else:
                            tw.wait(sy_xb, ct["x1T_done"])
                    if tb == 0:
                        for occ in (oc0, oc1):
                            sm, val = ct[f"{tag}_wld_oc{occ}"]
                            tw.wait(sm, val)
                    if idx >= 2:
                        tw.wait(v_all, ct[f"{tag}_ev_idx{idx - 2}"])
                    elif prev_ev is not None:
                        tw.wait(v_all, prev_ev)
                    b0, b1 = (0, 1) if idx % 2 == 0 else (2, 3)
                    r0, r1 = oc0 % 3, oc1 % 3
                    for e in range(c.EC):
                        stb = atres if stat_streamed else x1T
                        lhs = stb[:, e * c.TO + tb * 128:e * c.TO + (tb + 1) * 128]
                        st_, sp_ = (e == 0), (e == c.EC - 1)
                        t.matmul(psb[b0][:], lhs, wring[r0][:, e * 512:(e + 1) * 512],
                                 start=st_, stop=sp_)
                        mm = t.matmul(psb[b1][:], lhs, wring[r1][:, e * 512:(e + 1) * 512],
                                      start=st_, stop=sp_)
                    mm.then_inc(t_all, 1)
                    inc("t_all", 1)
                ct[f"{tag}_mm_idx{idx}"] = get("t_all")
                ct[f"{tag}_mm_oc{oc1}"] = get("t_all")
                ct[f"{tag}_mm_oc{oc0}"] = get("t_all")

                for oi in range(2):
                    oc = oc0 if oi == 0 else oc1
                    gx = xq[get("gv_local")]
                    inc("gv_local")
                    bk = ((0, 1) if idx % 2 == 0 else (2, 3))[oi]

                    @blk.vector
                    def _(v, idx=idx, tb=tb, oc=oc, gx=gx, bk=bk, is_i8=is_i8):
                        vw = VW(v)
                        vw.wait(t_all, ct[f"{tag}_mm_idx{idx}"])
                        vw.wait(ct[f"xload{gx}"][0], ct[f"xload{gx}"][1])
                        for sm, val in zt_guard.get(gx % NSL, []):
                            vw.wait(sm, val)
                        sev = KNOB['sev']
                        if sev == 1:
                            vw.tensor_copy(zt[gx % NSL][:], psb[bk][:])
                        elif sev == 2:
                            vw.tensor_tensor(zt[gx % NSL][:], psb[bk][:], xt[gx % NSL][:], op=ALU.add)
                        else:
                            if is_i8:
                                vw.wait(g_pre, scl_ready)
                                vw.tensor_tensor(zt[gx % NSL][:], psb[bk][:],
                                                 sclf[:, oc * 512:(oc + 1) * 512], op=ALU.mult)
                                vw.tensor_tensor(zt[gx % NSL][:], zt[gx % NSL][:], xt[gx % NSL][:],
                                                 op=ALU.add)
                            else:
                                vw.tensor_tensor(zt[gx % NSL][:], psb[bk][:], xt[gx % NSL][:], op=ALU.add)
                            vw.tensor_reduce(sqcol[:, 0:1], zt[gx % NSL][:], axis=AXX, op=ALU.add)
                            if oc == 0:
                                vw.tensor_copy(zsum[:, tb:tb + 1], sqcol[:, 0:1])
                            else:
                                vw.tensor_tensor(zsum[:, tb:tb + 1], zsum[:, tb:tb + 1],
                                                 sqcol[:, 0:1], op=ALU.add)
                    ct[f"{tag}_evhalf{gx}"] = get("v_all")
                    xt_guard[gx % NSL] = [(v_all, get("v_all"))]

                    if KNOB['sev'] == 0:
                        @blk.vector
                        def _(v, gx=gx, tb=tb, oc=oc):
                            vw = VW(v)
                            vw.tensor_tensor(xt[gx % NSL][:], zt[gx % NSL][:], zt[gx % NSL][:],
                                             op=ALU.mult)
                            vw.tensor_reduce(sqcol[:, 1:2], xt[gx % NSL][:], axis=AXX, op=ALU.add)
                            if oc == 0:
                                vw.tensor_copy(zsq[:, tb:tb + 1], sqcol[:, 1:2])
                            else:
                                vw.tensor_tensor(zsq[:, tb:tb + 1], zsq[:, tb:tb + 1],
                                                 sqcol[:, 1:2], op=ALU.add)
                        xt_guard[gx % NSL].append((v_all, get("v_all")))
                    ct[f"{tag}_sq{gx}"] = get("v_all")


                    @blk.sync
                    def _(sy, tb=tb, oc=oc, gx=gx):
                        sy.wait_ge(v_all, ct[f"{tag}_evhalf{gx}"])
                        sy.dma_start(out=zdst[tb * 128:(tb + 1) * 128, oc * 512:(oc + 1) * 512],
                                     in_=zt[gx % NSL][:]).then_inc(szs[gx % NSL], 16)
                        inc(f"szs{gx % NSL}", 16)
                    zt_guard[gx % NSL] = [(szs[gx % NSL], get(f"szs{gx % NSL}")),
                                          (v_all, get("v_all"))]
                    ct[f"{tag}_ev_idx{idx}"] = get("v_all")

                if idx + 2 < nidx:
                    emit_xload(idx + 2)
                if tb == c.OTB - 1:
                    for onext in (oc1 + 2, oc1 + 3):
                        if onext < c.OCN:
                            emit_wload(onext)
            ct[f"{tag}_z_done"] = [(szs[i], get(f"szs{i}")) for i in range(NSL)]

        def ln_finalize(tag, zsrc, x1_store, do_x1T):
            @blk.vector
            def _(v):
                vw = VW(v)
                vw.tensor_scalar(meanc[:, 0:c.OTB], zsum[:, 0:c.OTB], 1.0 / c.D, None, op0=ALU.mult)
                vw.tensor_scalar(rsig[:, 0:c.OTB], zsq[:, 0:c.OTB], 1.0 / c.D, None, op0=ALU.mult)
                vw.tensor_tensor(tmpc[:, 0:c.OTB], meanc[:, 0:c.OTB], meanc[:, 0:c.OTB], op=ALU.mult)
                vw.tensor_tensor(rsig[:, 0:c.OTB], rsig[:, 0:c.OTB], tmpc[:, 0:c.OTB],
                                 op=ALU.subtract)
                vw.tensor_scalar(rsig[:, 0:c.OTB], rsig[:, 0:c.OTB], 1e-5, None, op0=ALU.add)
            f1 = get("v_all")

            @blk.scalar
            def _(s, f1=f1):
                sw = SW(s)
                sw.wait(v_all, f1)
                sw.activation(rsig[:, 0:c.OTB], rsig[:, 0:c.OTB], AF.Sqrt)
            f2 = get("s_all")

            @blk.vector
            def _(v, f2=f2):
                vw = VW(v)
                vw.wait(s_all, f2)
                vw.reciprocal(rsig[:, 0:c.OTB], rsig[:, 0:c.OTB])
            stats_done = ("v", get("v_all"))

            for tb in range(c.OTB):
                for oc in range(c.OCN):
                    gx = get("gx")

                    @blk.scalar
                    def _(s, tb=tb, oc=oc, gx=gx):
                        for sm, val in xt_guard.get(gx % NSL, []):
                            s.wait_ge(sm, val)
                        for sm, val in ct[f"{tag}_z_done"]:
                            s.wait_ge(sm, val)
                        s.dma_start(out=xt[gx % NSL][:],
                                    in_=zsrc[tb * 128:(tb + 1) * 128,
                                             oc * 512:(oc + 1) * 512]).then_inc(scx[gx % NSL], 16)
                        inc(f"scx{gx % NSL}", 16)
                    ct[f"xload{gx}"] = (scx[gx % NSL], get(f"scx{gx % NSL}"))
                    inc("gx")

                    @blk.vector
                    def _(v, tb=tb, oc=oc, gx=gx, stats_done=stats_done, do_x1T=do_x1T):
                        vw = VW(v)
                        vw.wait(v_all if stats_done[0] == "v" else s_all, stats_done[1])
                        vw.wait(ct[f"xload{gx}"][0], ct[f"xload{gx}"][1])
                        for sm, val in zt_guard.get(gx % NSL, []):
                            vw.wait(sm, val)
                        if do_x1T and f"x1bf_tr{tb - 1}" in ct:
                            vw.wait(sy_xb, ct[f"x1bf_tr{tb - 1}"])
                        vw.tensor_scalar(zt[gx % NSL][:], xt[gx % NSL][:], meanc[:, tb:tb + 1],
                                         rsig[:, tb:tb + 1], op0=ALU.subtract, op1=ALU.mult)
                        if do_x1T:
                            vw.tensor_scalar(x1bf[:, oc * 512:(oc + 1) * 512], xt[gx % NSL][:],
                                             meanc[:, tb:tb + 1], rsig[:, tb:tb + 1],
                                             op0=ALU.subtract, op1=ALU.mult)
                    ct[f"pb_v{gx}"] = get("v_all")
                    xt_guard[gx % NSL] = [(v_all, get("v_all"))]

                    @blk.sync
                    def _(sy, tb=tb, oc=oc, gx=gx):
                        sy.wait_ge(v_all, ct[f"pb_v{gx}"])
                        sy.dma_start(out=x1_store[tb * 128:(tb + 1) * 128,
                                                  oc * 512:(oc + 1) * 512],
                                     in_=zt[gx % NSL][:]).then_inc(szs[gx % NSL], 16)
                        inc(f"szs{gx % NSL}", 16)
                    zt_guard[gx % NSL] = [(szs[gx % NSL], get(f"szs{gx % NSL}"))]

                if do_x1T:
                    last_v = get("v_all")

                    @blk.sync
                    def _(sy, tb=tb, last_v=last_v):
                        sy.wait_ge(v_all, last_v)
                        sy.dma_start_transpose(
                            out=x1T[:].rearrange("p (e f) -> p e f", e=c.EC)
                            [:, :, tb * 128:(tb + 1) * 128],
                            in_=x1bf[:].rearrange("p (e f) -> p e f", e=c.EC),
                        ).then_inc(sy_xb, 16)
                        inc("sy_xb", 16)
                    ct[f"x1bf_tr{tb}"] = get("sy_xb")
            if do_x1T:
                ct["x1T_done"] = get("sy_xb")

        linear_pass("A", woT, False, hs, z1d, True, [(sy_a2, a2a_stores)])
        pa.close()
        x1T = sb(p3, "x1T", [128, c.EC * c.TO], BF16)
        if KNOB['p3max'] >= 2:
            ln_finalize("A", z1d, x1d, True)
        ct["A_x1_done"] = [(szs[i], get(f"szs{i}")) for i in range(NSL)] + \
            [(sy_xb, get("sy_xb"))]
        if KNOB['p3max'] >= 3:
            linear_pass("C", wfT, True, x1d, z2d, False, ct["A_x1_done"],
                        prev_ev=ct["A_ev_idx%d" % (c.OCN // 2 * c.OTB - 1)])
        if KNOB['p3max'] >= 4:
            ln_finalize("C", z2d, out, False)

        @blk.sync
        def _(sy):
            for i in range(NSL):
                sy.wait_ge(szs[i], get(f"szs{i}"))
            sy.wait_ge(sy_xb, get("sy_xb"))
            if debug:
                sy.wait_ge(v_all, get("v_all"))
                sy.dma_start(out=dledd, in_=dled[:]).then_inc(sy_km, 16)
                inc("sy_km", 16)
                sy.dma_start(out=a2ad, in_=a2a_in.ap()).then_inc(sy_km, 16)
                inc("sy_km", 16)
                sy.wait_ge(sy_km, get("sy_km"))

        p3.close()

    return nc, ct


def _prep_inputs(c, hsf, wq, wk, wv, woTm, wfTm, sclm):
    pos = np.arange(c.S, dtype=np.float64)
    inv = 1.0 / (10000.0 ** (np.arange(0, c.ROT, 2, dtype=np.float64) / c.ROT))
    ang = pos[:, None] * inv[None, :]
    cosdm = np.ascontiguousarray(np.tile(np.cos(ang).astype(np.float32), (1, c.HPC)))
    sindm = np.ascontiguousarray(np.tile(np.sin(ang).astype(np.float32), (1, c.HPC)))

    nm = c.KC // 128
    m4 = np.zeros((nm, 128, c.KC), dtype=np.float32)
    r = np.arange(128)[:, None]
    cc = np.arange(c.KC)[None, :]
    for j in range(nm):
        m4[j] = np.where(cc <= j * 128 + r, 0.0, -1e9)
    m4 = m4.astype(ml_dtypes.bfloat16)

    maps = []
    for core in range(NCORES):
        maps.append({
            "hs": np.ascontiguousarray(hsf[core * c.TO:(core + 1) * c.TO]),
            "hsf": hsf,
            "wqT": np.ascontiguousarray(wq[core * c.DH:(core + 1) * c.DH].T),
            "wkT": np.ascontiguousarray(wk[core * c.DH:(core + 1) * c.DH].T),
            "wvT": np.ascontiguousarray(wv[core * c.DH:(core + 1) * c.DH].T),
            "woT": woTm, "wfT": wfTm, "scl": sclm,
            "cosd": cosdm, "sind": sindm, "m4d": m4,
        })
    return maps


def kernel(L0_attn_hidden_states, L0_attn_w_q, L0_attn_w_k, L0_attn_w_v,
           L0_attn_w_out, L0_ffn_weight, L0_ffn_scales):
    c = FULL
    hsf = np.asarray(L0_attn_hidden_states, dtype=np.float32).reshape(c.T, c.D)
    wq = np.asarray(L0_attn_w_q, dtype=np.float32)
    wk = np.asarray(L0_attn_w_k, dtype=np.float32)
    wv = np.asarray(L0_attn_w_v, dtype=np.float32)
    woTm = np.ascontiguousarray(np.asarray(L0_attn_w_out, dtype=np.float32).T)
    wfTm = np.ascontiguousarray(np.asarray(L0_ffn_weight).T)
    sclm = np.ascontiguousarray(np.asarray(L0_ffn_scales, dtype=np.float32).reshape(1, c.D))

    nc, _ = build_program(c)
    maps = _prep_inputs(c, hsf, wq, wk, wv, woTm, wfTm, sclm)
    res = run_bass_kernel_spmd(nc, maps, core_ids=list(range(NCORES)))
    full = np.concatenate([np.asarray(res.results[i]["out"]) for i in range(NCORES)], axis=0)
    return full.reshape(c.B, c.S, c.D)
